# version 18
# speedup vs baseline: 1.3051x; 1.0006x over previous
import numpy as np
import ml_dtypes
from contextlib import ExitStack

import concourse.bass as bass
import concourse.mybir as mybir
from concourse.bass_utils import run_bass_kernel_spmd

F32 = mybir.dt.float32
BF16 = mybir.dt.bfloat16
AF = mybir.ActivationFunctionType
ALU = mybir.AluOpType
AX = mybir.AxisListType
BF = ml_dtypes.bfloat16

B, N, D = 4, 4096, 1024
NC = 8
TOK = 2048
NT = 16
NHT = 5
ATT = NT + NHT
HEADS_L = 8
DH = 64
WIN = 512
NW = 4
HID = 2730
HIDP = 2816
KHID = HIDP // 128
G_HEADS = 16
KG = 8
NEG = -1e30
QT = 1024

DEBUG_STOP = None
PARANOID = 0
FAST = 0
RACECHECK = 0


def _pbar(ctx):
    for e in (ctx.t, ctx.v, ctx.s):
        i = e.nop()
    for e in (ctx.t, ctx.v, ctx.s):
        ctx.wait(e, "tp")
        ctx.wait(e, "vp")
        ctx.wait(e, "sp")



def _bucket_table(max_d):
    d = np.arange(max_d, dtype=np.int64)
    n = d.astype(np.float32)
    nf = np.maximum(n, np.float32(1.0))
    val = 16 + (
        np.log(nf / np.float32(16.0), dtype=np.float32)
        / np.float32(np.log(np.float32(128.0 / 16.0)))
        * np.float32(16)
    ).astype(np.int32)
    val = np.minimum(val, 31)
    return np.where(d < 16, d, val).astype(np.int64)


def _fm(x):
    T = x.shape[0]
    return np.ascontiguousarray(x.reshape(T, 8, 128).transpose(1, 2, 0))


def _rot_tables(t0):
    pos = (np.arange(-NHT * 128, NT * 128) + t0).astype(np.float32)
    half = np.arange(0, 64, 2, dtype=np.float32)
    inv_freq = (1.0 / 10000.0 ** (half / 64.0)).astype(np.float32)
    fr32 = pos[:, None] * inv_freq[None, :]
    freqs = np.concatenate([fr32, fr32], axis=1)
    base32 = ((half + 0.4 * 64.0) / (1.4 * 64.0)).astype(np.float32)
    power = (pos - 2048.0) / 256.0
    sc32 = base32[None, :] ** power[:, None]
    scale = np.concatenate([sc32, sc32], axis=1).astype(np.float32)
    sgn = np.concatenate([-np.ones(32), np.ones(32)]).astype(np.float32)
    cq = np.cos(freqs) * scale
    sq = sgn * np.sin(freqs) * scale
    ck = np.cos(freqs) / scale
    sk = sgn * np.sin(freqs) / scale
    return np.stack([cq, sq, ck, sk]).astype(BF)


def _prep_core(core, I):
    b, r = core // 2, core % 2
    t0 = r * TOK
    x = np.asarray(I["L0_attn_x"][b])
    d = {}
    d["x_fm"] = _fm(x[t0 : t0 + TOK])
    xh = np.zeros((NHT * 128 + 1, D), np.float32)
    if r == 1:
        xh[:] = x[t0 - NHT * 128 - 1 : t0]
    d["xh_fm"] = _fm(xh)

    wqkv = np.asarray(I["L0_ffn_wqkv"])
    d["wqkv_a"] = np.ascontiguousarray(
        wqkv.reshape(2, 8, 128, 3 * 512)
    )
    wo = np.asarray(I["L0_ffn_wo"])
    d["wo_a"] = np.ascontiguousarray(wo.reshape(2, 4, 128, 1024))
    w1 = np.asarray(I["L0_ffn_w1"])
    w1p = np.zeros((2, 1024, 2 * HIDP), np.float32)
    w1p[:, :, :HID] = w1[:, :, :HID]
    w1p[:, :, HIDP : HIDP + HID] = w1[:, :, HID:]
    d["w1_a"] = np.ascontiguousarray(w1p.reshape(2, 8, 128, 2 * HIDP))
    w2 = np.asarray(I["L0_ffn_w2"])
    w2p = np.zeros((2, HIDP, 1024), np.float32)
    w2p[:, :HID] = w2
    d["w2_a"] = np.ascontiguousarray(w2p.reshape(2, KHID, 128, 1024))
    b1 = np.asarray(I["L0_ffn_b1"])
    b1p = np.zeros((2, 2 * HIDP), np.float32)
    b1p[:, :HID] = b1[:, :HID]
    b1p[:, HIDP : HIDP + HID] = b1[:, HID:]
    d["b1_a"] = np.ascontiguousarray(
        b1p.reshape(2, 2 * KHID, 128).transpose(2, 0, 1)
    )
    d["b2_a"] = np.ascontiguousarray(
        np.asarray(I["L0_ffn_b2"]).reshape(2, 8, 128).transpose(2, 0, 1)
    )
    d["lgb_a"] = np.ascontiguousarray(
        np.stack([np.asarray(I["L0_ffn_ln_g"]), np.asarray(I["L0_ffn_ln_b"])])
        .reshape(2, 2, 8, 128)
        .transpose(3, 1, 0, 2)
    )
    d["qs_bc"] = np.ascontiguousarray(
        np.broadcast_to(np.asarray(I["L0_ffn_q_scale"])[:, None, :], (2, 128, 64))
        .transpose(1, 0, 2)
    )
    d["ks_bc"] = np.ascontiguousarray(
        np.broadcast_to(np.asarray(I["L0_ffn_k_scale"])[:, None, :], (2, 128, 64))
        .transpose(1, 0, 2)
    )

    kept = list(range(KG * r, KG * r + KG))
    other = [h for h in range(G_HEADS) if h not in kept]
    Wq = np.asarray(I["L0_attn_Wq"])
    qcols = np.concatenate([np.arange(64 * h, 64 * h + 64) for h in kept])
    d["wqg_a"] = np.ascontiguousarray(Wq[:, qcols].reshape(8, 128, KG * 64))
    d["wkvg_a"] = np.ascontiguousarray(
        np.asarray(I["L0_attn_Wkv"]).reshape(8, 128, 128)
    )
    Wout = np.asarray(I["L0_attn_Wout"])
    rperm = np.concatenate(
        [np.arange(64 * h, 64 * h + 64) for h in kept + other]
    )
    d["wog_a"] = np.ascontiguousarray(Wout[rperm].reshape(8, 128, 1024))
    d["boutg_a"] = np.ascontiguousarray(
        np.asarray(I["L0_attn_bout"]).reshape(8, 128).T
    )

    rel = np.asarray(I["L0_attn_rel_emb"])
    ki = np.arange(128)
    qi = np.arange(128)
    dist = (qi[None, None, :] + 128 * np.arange(2)[:, None, None]
            - ki[None, :, None])
    bt = _bucket_table(4096)
    distc = np.clip(dist, 0, None)
    gath = rel[bt[distc]][:, :, :, kept]
    d["corr_rel"] = np.ascontiguousarray(gath.transpose(1, 3, 0, 2))
    d["rel31_bc"] = np.ascontiguousarray(
        np.broadcast_to(rel[31, kept][None, :], (128, KG)).astype(np.float32)
    )
    d["cmask"] = np.ascontiguousarray(
        np.where(dist < 0, np.float32(NEG), np.float32(0.0)).transpose(1, 0, 2)
    )

    tri_prev = np.where(ki[:, None] >= qi[None, :], 0.0, NEG).astype(BF)
    tri_diag = np.where(ki[:, None] <= qi[None, :], 0.0, NEG).astype(BF)
    d["tri"] = np.ascontiguousarray(
        np.stack([tri_prev, tri_diag]).transpose(1, 0, 2)
    )

    pr = np.ones((1, ATT * 128), np.float32)
    if r == 0:
        pr[0, : NHT * 128] = 1e30
    d["poison_row"] = pr

    d["rot_t"] = _rot_tables(t0)
    d["ident"] = np.eye(128, dtype=np.float32)

    d["role_col"] = np.full((128, 1), float(r), np.float32)
    d["nrole_col"] = np.full((128, 1), float(1 - r), np.float32)
    d["ones_col"] = np.ones((128, 1), np.float32)
    return d




class _RecInstr:
    __slots__ = ("entry",)

    def __init__(self, entry):
        self.entry = entry

    def then_inc(self, sem, val=1):
        self.entry[3].append((sem, val))
        return self


class _RecEng:

    def __init__(self):
        self.ops = []

    def __getattr__(self, meth):
        if meth.startswith("_"):
            raise AttributeError(meth)

        def call(*a, **k):
            entry = [meth, a, k, []]
            self.ops.append(entry)
            return _RecInstr(entry)

        return call

    def replay(self, eng):
        for meth, a, k, incs in self.ops:
            r = getattr(eng, meth)(*a, **k)
            for sem, val in incs:
                r = r.then_inc(sem, val)


class Sem:
    def __init__(self, handle):
        self.h = handle
        self.n = 0


CONST_VALS = [1e-5, 0.125, 1.0, 0.5, 8.0, -1.0, -0.5, 1e-24]


class Ctx:

    def __init__(self, nc):
        self.nc = nc
        self.t = _RecEng()
        self.v = _RecEng()
        self.s = _RecEng()
        self.g = _RecEng()
        self.q = _RecEng()
        self.sems = {}
        self.sb = {}

    def sem(self, name):
        return self.sems[name]

    def inc(self, instr, name, k=1):
        s = self.sems[name]
        s.n += k
        instr.then_inc(s.h, k)

    _OWN = {"t": "tp", "v": "vp", "s": "sp", "g": "gp"}

    def wait(self, eng, name, n=None):
        if FAST:
            for k, own in self._OWN.items():
                if eng is getattr(self, k) and name == own:
                    return
        s = self.sems[name]
        eng.wait_ge(s.h, s.n if n is None else n)


def _dma(ctx, eng, semname, out, in_):
    i = eng.dma_start(out=out, in_=in_)
    ctx.inc(i, semname, 16)
    return i




def ph_load_const(ctx, din):
    sb, q, g = ctx.sb, ctx.q, ctx.g
    for c in range(8):
        _dma(ctx, q, "qd", sb["xf"][:, c, 1 : 1 + TOK], din["x_fm"].ap()[c])
    _dma(ctx, q, "qd", sb["lgb"][:], din["lgb_a"].ap())
    _dma(ctx, q, "qd", sb["b1c"][:], din["b1_a"].ap())
    _dma(ctx, q, "qd", sb["b2c"][:], din["b2_a"].ap())
    _dma(ctx, q, "qd", sb["boutc"][:], din["boutg_a"].ap())
    _dma(ctx, q, "qd", sb["qs"][:], din["qs_bc"].ap())
    _dma(ctx, q, "qd", sb["ks"][:], din["ks_bc"].ap())
    _dma(ctx, q, "qd", sb["rel31"][:], din["rel31_bc"].ap())
    _dma(ctx, q, "qd", sb["role"][:], din["role_col"].ap())
    _dma(ctx, q, "qd", sb["nrole"][:], din["nrole_col"].ap())
    _dma(ctx, q, "qd", sb["sb_tri"][:], din["tri"].ap())
    _dma(ctx, g, "gd", sb["sb_ident"][:], din["ident"].ap())
    _dma(ctx, g, "gd", sb["onesb"][:], din["ones_col"].ap())


def _ln_stats(ctx, src_bf_chunks, T, psA, psB, stat, sq):
    t, v, s = ctx.t, ctx.v, ctx.s
    sb = ctx.sb
    onesb = sb["onesb"]
    nslice = (T + 511) // 512
    ctx.wait(v, "vp")
    ctx.wait(t, "gd")
    for c in range(8):
        i = v.tensor_mul(sq[:, 0:T], src_bf_chunks[c], src_bf_chunks[c])
        ctx.inc(i, "vp")
        ctx.wait(v, "vp")
        ctx.wait(t, "vp")
        for ns in range(nslice):
            w = min(512, T - ns * 512)
            sl = slice(ns * 512, ns * 512 + w)
            i = t.matmul(psA[0:1, sl], onesb[:], src_bf_chunks[c][:, sl],
                         start=(c == 0), stop=(c == 7))
            i = t.matmul(psB[0:1, sl], onesb[:], sq[:, 0:T][:, sl],
                         start=(c == 0), stop=(c == 7))
            ctx.inc(i, "tp")
        if c < 7:
            ctx.wait(v, "tp")
    ctx.wait(v, "tp")
    inv = 1.0 / 1024.0
    mean = psA[32:33, 0:T]
    m2 = stat[0:1, 1, 0:T]
    var = psB[32:33, 0:T]
    i = v.tensor_scalar(mean, psA[0:1, 0:T], inv, None, op0=ALU.mult)
    ctx.inc(i, "vp")
    i = v.tensor_scalar(var, psB[0:1, 0:T], inv, None, op0=ALU.mult)
    ctx.inc(i, "vp")
    ctx.wait(s, "vp")
    i = s.activation(m2, mean, AF.Square)
    ctx.inc(i, "sp")
    ctx.wait(v, "sp")
    i = v.tensor_sub(var, var, m2)
    ctx.inc(i, "vp")
    ctx.wait(s, "vp")
    i = s.activation(var, var, AF.Ln, bias=1e-5)
    ctx.inc(i, "sp")
    srow = stat[0:1, 0, 0:T]
    i = s.activation(srow, var, AF.Exp, scale=-0.5)
    ctx.inc(i, "sp")
    ctx.wait(v, "sp")
    msrow = stat[0:1, 1, 0:T]
    i = v.tensor_mul(msrow, mean, srow)
    ctx.inc(i, "vp")
    ctx.wait(v, "vp")
    i = v.tensor_scalar(msrow, msrow, -1.0, None, op0=ALU.mult)
    ctx.inc(i, "vp")
    return srow, msrow


def _ln_bcast(ctx, srow, msrow, T, psA, psB, s_bc, ms_bc):
    t, v = ctx.t, ctx.v
    onesb = ctx.sb["onesb"]
    ones_row = ctx.sb["ones_row"]
    nslice = (T + 511) // 512
    ctx.wait(t, "vp")
    for ns in range(nslice):
        w = min(512, T - ns * 512)
        sl = slice(ns * 512, ns * 512 + w)
        i = t.matmul(psA[:, sl], ones_row[:], srow[:, sl], start=True, stop=True)
        i = t.matmul(psB[:, sl], ones_row[:], msrow[:, sl], start=True, stop=True)
    ctx.inc(i, "tp")
    ctx.wait(v, "tp")
    i = v.tensor_copy(s_bc[:, 0:T], psA[:, 0:T])
    ctx.inc(i, "vp")
    i = v.tensor_copy(ms_bc[:, 0:T], psB[:, 0:T])
    ctx.inc(i, "vp")
    ctx.wait(v, "vp")


def ph_outer_ln1(ctx, din, with_halo=True, spill=None, tag="ln"):
    nc, sb = ctx.nc, ctx.sb
    t, v, s, g, q = ctx.t, ctx.v, ctx.s, ctx.g, ctx.q
    psA = sb["psA"]
    psB = sb["psB"]
    with nc.sbuf_tensor(f"{tag}_sbc", [128, TOK], F32) as s_bc, \
         nc.sbuf_tensor(f"{tag}_msbc", [128, TOK], F32) as ms_bc, \
         nc.sbuf_tensor(f"{tag}_xbf", [128, 8, TOK], BF16) as xbf, \
         nc.sbuf_tensor(f"{tag}_stat", [1, 4, TOK], F32) as stat, \
         nc.sbuf_tensor(f"{tag}_sq", [128, TOK], BF16) as sq, \
         nc.sbuf_tensor(f"{tag}_tmp", [128, TOK], F32) as tmp:
        ctx.wait(v, "qd")
        ctx.wait(g, "gd")
        if spill is not None:
            ctx.wait(q, "vp")
            ctx.wait(q, "tp")
            for c in range(8):
                _dma(ctx, ctx.q, "qd", spill.ap()[c], sb["xf"][:, c, 1 : 1 + TOK])
            ctx.wait(v, "qd")
        ctx.wait(v, "vp")
        ctx.wait(v, "tp")
        ctx.wait(v, "sp")
        for c in range(8):
            i = v.tensor_copy(xbf[:, c, :], sb["xf"][:, c, 1 : 1 + TOK])
            ctx.inc(i, "vp")
        srow, msrow = _ln_stats(
            ctx, [xbf[:, c, :] for c in range(8)], TOK,
            psA, psB, stat, sq,
        )
        _ln_bcast(ctx, srow, msrow, TOK, psA, psB, s_bc, ms_bc)
        ctx.wait(v, "vp")
        for c in range(8):
            xc = sb["xf"][:, c, 1 : 1 + TOK]
            i = v.tensor_mul(tmp[:], xc, s_bc[:])
            ctx.inc(i, "vp")
            ctx.wait(v, "vp")
            i = v.tensor_add(xc, tmp[:], ms_bc[:])
            ctx.inc(i, "vp")
            ctx.wait(v, "vp")
        if not with_halo:
            return
        TH = NHT * 128 + 1
        for c in range(8):
            ctx.wait(q, "vp")
            _dma(ctx, q, "qd", tmp[:, 0:TH], din["xh_fm"].ap()[c])
            ctx.wait(v, "qd")
            i = v.tensor_copy(xbf[:, c, 0:TH], tmp[:, 0:TH])
            ctx.inc(i, "vp")
        srow, msrow = _ln_stats(
            ctx, [xbf[:, c, 0:TH] for c in range(8)], TH,
            psA, psB, stat, sq,
        )
        _ln_bcast(ctx, srow, msrow, TH, psA, psB, s_bc, ms_bc)
        ctx.wait(v, "vp")
        for c in range(8):
            i = v.tensor_mul(tmp[:, 0:TH], xbf[:, c, 0:TH], s_bc[:, 0:TH])
            ctx.inc(i, "vp")
            ctx.wait(v, "vp")
            i = v.tensor_add(sb["slab"][:, c, 0:TH], tmp[:, 0:TH], ms_bc[:, 0:TH])
            ctx.inc(i, "vp")
            ctx.wait(v, "vp")
        ctx.wait(v, "vp")
        i = v.tensor_copy(sb["xf"][:, :, 0:1], sb["slab"][:, :, TH - 1 : TH])
        ctx.inc(i, "vp")


def ph_debug_dump(ctx, dout, which="xf"):
    if DEBUG_STOP and DEBUG_STOP.endswith("o"):
        ctx.wait(ctx.q, "gd")
        return
    sb, q, v = ctx.sb, ctx.q, ctx.v
    ctx.wait(q, "vp")
    ctx.wait(q, "tp")
    ctx.wait(q, "sp")
    for c in range(8):
        _dma(ctx, q, "qd", dout.ap()[c], sb["xf"][:, c, 1 : 1 + TOK])
    ctx.wait(q, "qd")


def ph_store_out(ctx, dout):
    sb, q = ctx.sb, ctx.q
    ctx.wait(q, "vp")
    for c in range(8):
        _dma(ctx, q, "qd", dout.ap()[c], sb["xf"][:, c, 1 : 1 + TOK])
    ctx.wait(q, "qd")


def ph_call_end(ctx, din, x0_dram_key, dram, call_idx):
    nc, sb = ctx.nc, ctx.sb
    v, q = ctx.v, ctx.q
    src = (din[x0_dram_key].ap() if x0_dram_key in ("x_fm",)
           else dram[x0_dram_key].ap())
    with nc.sbuf_tensor(f"ce{call_idx}_x0", [128, 2, TOK], F32) as x0b:
        ctx.wait(q, "vp")
        ctx.wait(q, "tp")
        for c in range(8):
            _dma(ctx, q, "qd", x0b[:, c % 2, :], src[c])
            ctx.wait(v, "qd")
            ctx.wait(v, "vp")
            xc = sb["xf"][:, c, 1 : 1 + TOK]
            i = v.scalar_tensor_tensor(xc, xc, 0.5, x0b[:, c % 2, :],
                                       op0=ALU.mult, op1=ALU.add)
            ctx.inc(i, "vp")
            if c < 7:
                ctx.wait(q, "vp")




def ph_global(ctx, din, dram):
    nc, sb = ctx.nc, ctx.sb
    t, v, s, g, q = ctx.t, ctx.v, ctx.s, ctx.g, ctx.q
    psA, psB = sb["psA"], sb["psB"]
    xf = sb["xf"]

    ctx.wait(g, "vp")
    ctx.wait(g, "tp")
    i = g.dma_start(out=dram["x1_snd"].ap(), in_=xf[:, :, 1 : 1 + TOK])
    ctx.inc(i, "gd", 16)
    ctx.wait(g, "gd")
    i = g.collective_compute(
        "AllGather", ALU.bypass,
        replica_groups=[[0, 1], [2, 3], [4, 5], [6, 7]],
        ins=[dram["x1_snd"].ap().opt()],
        outs=[dram["x1_rcv"].ap().opt()],
    )
    ctx.inc(i, "cc")
    ctx.wait(q, "cc")
    ctx.wait(g, "cc")

    with nc.sbuf_tensor("g_qgpair", [128, 4, 32, 128], BF16) as qgpair, \
         nc.sbuf_tensor("g_k65", [128, 4096], BF16) as k65g, \
         nc.sbuf_tensor("g_v66", [128, 32, 65], BF16) as v66g, \
         nc.sbuf_tensor("g_ofull", [128, 4, 32, 128], BF16) as og_full:

        ctx.wait(v, "vp")
        ctx.wait(v, "tp")
        ctx.wait(v, "qd")
        i = v.memset(v66g[:, :, 64:65], 1.0)
        ctx.inc(i, "vp")
        i = v.memset(k65g[64:65, :], 1.0)
        ctx.inc(i, "vp")

        with nc.sbuf_tensor("g_x1g", [128, 8, TOK], BF16) as x1g, \
             nc.sbuf_tensor("g_wq", [128, 8, KG * 64], BF16) as wqg_b, \
             nc.sbuf_tensor("g_wkv", [128, 8, 128], BF16) as wkvg_b, \
             nc.sbuf_tensor("g_sq", [128, TOK], BF16) as gsq, \
             nc.sbuf_tensor("g_rows", [1, 2, TOK], BF16) as grows:
            ctx.wait(g, "sp")
            ctx.wait(q, "gd", ctx.wbf_done)
            ctx.wait(q, "tp")
            ctx.wait(q, "vp")
            ctx.wait(q, "sp")
            i = q.dma_start(out=wqg_b[:],
                            in_=ctx.wbf["wqg_a"].ap().rearrange("c p j -> p c j"))
            ctx.inc(i, "qd", 16)
            i = q.dma_start(out=wkvg_b[:],
                            in_=ctx.wbf["wkvg_a"].ap().rearrange("c p j -> p c j"))
            ctx.inc(i, "qd", 16)
            for half in range(2):
                ctx.wait(q, "vp")
                ctx.wait(q, "tp")
                for c in range(8):
                    _dma(ctx, q, "qd", x1g[:, c, :], dram["x1_rcv"].ap()[half][:, c, :])
                ctx.wait(v, "qd")
                for c in range(8):
                    ctx.wait(v, "tp")
                    i = v.tensor_mul(gsq[:], x1g[:, c, :], x1g[:, c, :])
                    ctx.inc(i, "vp")
                    ctx.wait(t, "vp")
                    ctx.wait(t, "qd")
                    for ns in range(4):
                        sl = slice(ns * 512, ns * 512 + 512)
                        i = t.matmul(psA[0:1, sl], sb["onesb"][:], x1g[:, c, sl],
                                     start=(c == 0), stop=(c == 7))
                        i = t.matmul(psB[0:1, sl], sb["onesb"][:], gsq[:, sl],
                                     start=(c == 0), stop=(c == 7))
                        ctx.inc(i, "tp")
                ctx.wait(v, "tp")
                inv = 1.0 / 1024.0
                mean = psA[32:33, 0:TOK]
                var = psB[32:33, 0:TOK]
                m2 = grows[0:1, 1, :]
                i = v.tensor_scalar(mean, psA[0:1, 0:TOK], inv, None, op0=ALU.mult)
                ctx.inc(i, "vp")
                i = v.tensor_scalar(var, psB[0:1, 0:TOK], inv, None, op0=ALU.mult)
                ctx.inc(i, "vp")
                ctx.wait(s, "vp")
                i = s.activation(m2, mean, AF.Square)
                ctx.inc(i, "sp"); ctx.wait(v, "sp")
                i = v.tensor_sub(var, var, m2)
                ctx.inc(i, "vp")
                ctx.wait(s, "vp")
                i = s.activation(var, var, AF.Ln, bias=1e-5)
                ctx.inc(i, "sp")
                srf = psA[64:65, 0:TOK]
                i = s.activation(srf, var, AF.Exp, scale=-0.5)
                ctx.inc(i, "sp"); ctx.wait(v, "sp")
                i = v.tensor_copy(grows[0:1, 0, :], srf)
                ctx.inc(i, "vp"); ctx.wait(v, "vp")
                i = v.tensor_mul(grows[0:1, 1, :], mean, grows[0:1, 0, :])
                ctx.inc(i, "vp"); ctx.wait(v, "vp")
                i = v.tensor_scalar_mul(grows[0:1, 1, :], grows[0:1, 1, :], -1.0)
                ctx.inc(i, "vp")
                ctx.wait(t, "vp")
                for ns in range(4):
                    sl = slice(ns * 512, ns * 512 + 512)
                    i = t.matmul(psB[:, sl], sb["ones_row_bf"][:], grows[0:1, 1, sl],
                                 start=True, stop=True)
                    i = t.matmul(psA[:, sl], sb["ones_row_bf"][:], grows[0:1, 0, sl],
                                 start=True, stop=True)
                ctx.inc(i, "tp")
                ctx.wait(v, "tp")
                for c in range(8):
                    i = v.tensor_mul(x1g[:, c, :], x1g[:, c, :], psA[:, 0:TOK])
                    ctx.inc(i, "vp")
                    ctx.wait(v, "vp")
                    i = v.tensor_add(x1g[:, c, :], x1g[:, c, :], psB[:, 0:TOK])
                    ctx.inc(i, "vp")
                    ctx.wait(v, "vp")
                ctx.wait(t, "vp")
                ctx.wait(t, "gd")
                for tt in range(16):
                    tg = half * 16 + tt
                    for c in range(8):
                        stat = x1g[:, c, tt * 128 : (tt + 1) * 128]
                        i = t.matmul(psB[:, 0:512], stat, wqg_b[:, c, :],
                                     start=(c == 0), stop=(c == 7))
                        i = t.matmul(psB[:, 512:640], stat, wkvg_b[:, c, :],
                                     start=(c == 0), stop=(c == 7))
                    ctx.inc(i, "tp")
                    ctx.wait(v, "tp")
                    qtmp = gsq[:, 0:512]
                    ktmp = gsq[:, 512:576]
                    i = v.tensor_copy(qtmp, psB[:, 0:512])
                    ctx.inc(i, "vp")
                    i = v.tensor_copy(ktmp, psB[:, 512:576])
                    ctx.inc(i, "vp")
                    i = v.tensor_copy(v66g[:, tg, 0:64],
                                      psB[:, 576:640])
                    ctx.inc(i, "vp")
                    ctx.wait(v, "vp")
                    ctx.wait(t, "vp")
                    pstq = psA[:, 0:256].bitcast(BF16)
                    for p in range(4):
                        i = t.matmul(pstq[:, p * 128 : (p + 1) * 128],
                                     qtmp[:, p * 128 : (p + 1) * 128] if False else gsq[:, p * 128 : (p + 1) * 128],
                                     sb["sb_ident"][:],
                                     start=(p == 0), stop=(p == 3), is_transpose=True)
                    ctx.inc(i, "tp")
                    pstk = psA[:, 256:320].bitcast(BF16)
                    i = t.matmul(pstk[0:64, :], gsq[:, 512:576], sb["sb_ident"][:],
                                 start=True, stop=True, is_transpose=True)
                    ctx.inc(i, "tp")
                    ctx.wait(v, "tp")
                    i = v.tensor_copy(qgpair[:, :, tg, :],
                                      pstq.rearrange("p (a b) -> p a b", a=4))
                    ctx.inc(i, "vp")
                    i = v.tensor_copy(k65g[0:64, tg * 128 : (tg + 1) * 128],
                                      pstk[0:64, :])
                    ctx.inc(i, "vp")
                    ctx.wait(t, "vp")

        with nc.sbuf_tensor("g_q65", [128, 2, 4096], BF16) as q65wg, \
             nc.sbuf_tensor("g_qabs", [128, 32 * 128], BF16) as qabsg, \
             nc.sbuf_tensor("g_Kg", [128, 1], BF16) as Kg2, \
             nc.sbuf_tensor("g_pt", [128, 2048], BF16) as PT, \
             nc.sbuf_tensor("g_rec", [1, 512], F32) as rec, \
             nc.sbuf_tensor("g_recb", [128, 512], F32) as recbg, \
             nc.sbuf_tensor("g_corr", [128, KG, 2, 128], BF16) as corrm:

            with nc.sbuf_tensor("g_crel", [128, KG, 2, 128], F32) as crel, \
                 nc.sbuf_tensor("g_cmask", [128, 2, 128], F32) as cmask:
                ctx.wait(q, "vp")
                ctx.wait(q, "tp")
                _dma(ctx, q, "qd", crel[:], din["corr_rel"].ap())
                _dma(ctx, q, "qd", cmask[:], din["cmask"].ap())
                ctx.wait(v, "qd")
                i = v.tensor_sub(crel[:], crel[:],
                                 sb["rel31"][:].unsqueeze(2).unsqueeze(3)
                                 .broadcast_to([128, KG, 2, 128]))
                ctx.inc(i, "vp"); ctx.wait(v, "vp")
                i = v.tensor_scalar_mul(crel[:], crel[:], 64.0)
                ctx.inc(i, "vp"); ctx.wait(v, "vp")
                i = v.tensor_add(corrm[:], crel[:],
                                 cmask[:].unsqueeze(1).broadcast_to([128, KG, 2, 128]))
                ctx.inc(i, "vp"); ctx.wait(v, "vp")

            i = v.tensor_reduce(Kg2[0:64, :], k65g[0:64, :], axis=AX.X,
                                op=ALU.max, apply_absolute_value=True)
            ctx.inc(i, "vp")
            i = v.tensor_reduce(Kg2[64:128, :], k65g[0:64, :], axis=AX.X,
                                op=ALU.max, apply_absolute_value=True)
            ctx.inc(i, "vp")

            for p in range(4):
                ctx.wait(v, "tp")
                ctx.wait(v, "sp")
                ctx.wait(v, "vp")
                ctx.wait(s, "tp")
                ctx.wait(s, "vp")
                i = s.activation(qabsg[:], qgpair[:, p, :, :], AF.Abs)
                ctx.inc(i, "sp")
                ctx.wait(t, "sp")
                for hh in range(2):
                    for t32 in range(32):
                        pbb = (psA[0:1, t32 * 128 : (t32 + 1) * 128] if t32 < 16
                               else psB[0:1, (t32 - 16) * 128 : (t32 - 15) * 128])
                        i = t.matmul(pbb,
                                     Kg2[64 * hh : 64 * hh + 64, :],
                                     qabsg[64 * hh : 64 * hh + 64,
                                           t32 * 128 : (t32 + 1) * 128],
                                     start=True, stop=True)
                        ctx.inc(i, "tp")
                    ctx.wait(v, "tp")
                    i = v.tensor_scalar_mul(q65wg[64:65, hh, 0:2048],
                                            psA[0:1, 0:2048], -1.0)
                    ctx.inc(i, "vp")
                    i = v.tensor_scalar_mul(q65wg[64:65, hh, 2048:4096],
                                            psB[0:1, 0:2048], -1.0)
                    ctx.inc(i, "vp")
                    ctx.wait(t, "vp")
                i = v.tensor_copy(q65wg[0:64, 0, :], qgpair[0:64, p, :, :])
                ctx.inc(i, "vp")
                i = v.tensor_copy(q65wg[0:64, 1, :], qgpair[64:128, p, :, :])
                ctx.inc(i, "vp")
                ctx.wait(t, "vp")

                for hh in range(2):
                    hgl = 2 * p + hh
                    for gq in range(8):
                        qc0 = gq * 512
                        nkb = 4 * (gq + 1)
                        av_first = True
                        for b0 in range(0, nkb, 4):
                            batch = list(range(b0, min(b0 + 4, nkb)))
                            ctx.wait(t, "sp")
                            for bi, kb in enumerate(batch):
                                i = t.matmul(
                                    psA[:, bi * 512 : bi * 512 + 512],
                                    k65g[0:65, kb * 128 : (kb + 1) * 128],
                                    q65wg[0:65, hh, qc0 : qc0 + 512],
                                    start=True, stop=True)
                            ctx.inc(i, "tp")
                            ctx.wait(v, "tp")
                            ncorr = 0
                            for bi, kb in enumerate(batch):
                                for tloc in range(4):
                                    qt_g = 4 * gq + tloc
                                    if kb == qt_g:
                                        tc = 0
                                    elif kb == qt_g - 1:
                                        tc = 1
                                    else:
                                        continue
                                    sl = psA[:, bi * 512 + tloc * 128 :
                                             bi * 512 + tloc * 128 + 128]
                                    i = v.tensor_add(sl, sl, corrm[:, hgl, tc, :])
                                    ctx.inc(i, "vp")
                                    ncorr += 1
                            if ncorr:
                                ctx.wait(s, "vp")
                            ctx.wait(s, "tp")
                            bw = len(batch) * 512
                            i = s.activation(PT[:, 0:bw], psA[:, 0:bw], AF.Exp,
                                             scale=0.125)
                            ctx.inc(i, "sp")
                            ctx.wait(t, "sp")
                            for bi, kb in enumerate(batch):
                                last = kb == nkb - 1
                                a0_ = max(0, (kb - 4 * gq) * 128)
                                i = t.matmul(psB[0:65, a0_:512],
                                             v66g[:, kb, :],
                                             PT[:, bi * 512 + a0_ : bi * 512 + 512],
                                             start=av_first, stop=last)
                                av_first = False
                            ctx.inc(i, "tp")
                        ctx.wait(s, "tp")
                        i = s.activation(rec[:, 0:512], psB[64:65, 0:512], AF.Ln)
                        ctx.inc(i, "sp")
                        i = s.activation(rec[:, 0:512], rec[:, 0:512], AF.Exp,
                                         scale=-1.0)
                        ctx.inc(i, "sp")
                        ctx.wait(t, "sp")
                        i = t.matmul(psB[0:64, 512:1024],
                                     sb["ones_row"][0:1, 0:64], rec[0:1, 0:512],
                                     start=True, stop=True)
                        ctx.inc(i, "tp")
                        ctx.wait(v, "tp")
                        i = v.tensor_copy(recbg[0:64, :], psB[0:64, 512:1024])
                        ctx.inc(i, "vp")
                        ctx.wait(v, "vp")
                        i = v.tensor_mul(
                            og_full[64 * hh : 64 * hh + 64, p, 4 * gq : 4 * gq + 4, :]
                            .rearrange("p a b -> p (a b)"),
                            psB[0:64, 0:512], recbg[0:64, :])
                        ctx.inc(i, "vp")
                        ctx.wait(t, "vp")

        with nc.sbuf_tensor("g_osnd", [128, 4, TOK], BF16) as osnd, \
             nc.sbuf_tensor("g_opart", [128, 4, TOK], BF16) as opart, \
             nc.sbuf_tensor("g_wout", [128, 8, 1024], BF16) as wog_b:
            ctx.wait(v, "vp")
            ctx.wait(v, "tp")
            i = v.tensor_scalar_mul(
                osnd[:].rearrange("p a (c b) -> p a c b", c=16),
                og_full[:, :, 0:16, :],
                sb["role"][:])
            ctx.inc(i, "vp"); ctx.wait(v, "vp")
            i = v.scalar_tensor_tensor(
                osnd[:].rearrange("p a (c b) -> p a c b", c=16),
                og_full[:, :, 16:32, :],
                sb["nrole"][:],
                osnd[:].rearrange("p a (c b) -> p a c b", c=16),
                op0=ALU.mult, op1=ALU.add)
            ctx.inc(i, "vp")
            ctx.wait(g, "vp")
            i = g.dma_start(out=dram["og_snd"].ap(), in_=osnd[:])
            ctx.inc(i, "gd", 16)
            ctx.wait(g, "gd")
            i = g.collective_compute(
                "AllGather", ALU.bypass,
                replica_groups=[[0, 1], [2, 3], [4, 5], [6, 7]],
                ins=[dram["og_snd"].ap().opt()],
                outs=[dram["og_rcv"].ap().opt()],
            )
            ctx.inc(i, "cc")
            ctx.wait(q, "cc")
            ctx.wait(v, "cc")
            i = v.tensor_scalar_mul(
                og_full[:, :, 0:16, :], og_full[:, :, 0:16, :], sb["nrole"][:])
            ctx.inc(i, "vp"); ctx.wait(v, "vp")
            i = v.scalar_tensor_tensor(
                og_full[:, :, 0:16, :], og_full[:, :, 16:32, :], sb["role"][:],
                og_full[:, :, 0:16, :], op0=ALU.mult, op1=ALU.add)
            ctx.inc(i, "vp")
            ctx.wait(q, "vp")
            for c in range(4):
                _dma(ctx, q, "qd", opart[:, c, :], dram["og_rcv"].ap()[0][:, c, :])
            ctx.wait(v, "qd")
            i = v.tensor_scalar_mul(opart[:].rearrange("p a b -> p (a b)"),
                                    opart[:].rearrange("p a b -> p (a b)"),
                                    sb["role"][:])
            ctx.inc(i, "vp")
            ctx.wait(q, "vp")
            for c in range(4):
                _dma(ctx, q, "qd", osnd[:, c, :], dram["og_rcv"].ap()[1][:, c, :])
            ctx.wait(v, "qd")
            ctx.wait(v, "vp")
            i = v.scalar_tensor_tensor(
                opart[:].rearrange("p a b -> p (a b)"),
                osnd[:].rearrange("p a b -> p (a b)"),
                sb["nrole"][:],
                opart[:].rearrange("p a b -> p (a b)"),
                op0=ALU.mult, op1=ALU.add)
            ctx.inc(i, "vp")
            ctx.wait(g, "sp")
            ctx.wait(g, "tp")
            ctx.wait(q, "tp")
            ctx.wait(q, "vp")
            ctx.wait(q, "sp")
            i = q.dma_start(out=wog_b[:],
                            in_=ctx.wbf["wog_a"].ap().rearrange("c p j -> p c j"))
            ctx.inc(i, "qd", 16)
            ctx.wait(t, "qd")
            ctx.wait(t, "vp")
            for m in range(8):
                for ng in range(4):
                    pb = psA[:, (ng % 2) * 512 : (ng % 2) * 512 + 512]
                    nsl = slice(ng * 512, ng * 512 + 512)
                    for kc in range(8):
                        mov = (og_full[:, kc, 4 * ng : 4 * ng + 4, :]
                               if kc < 4 else
                               opart[:, kc - 4, :][:, nsl])
                        i = t.matmul(pb, wog_b[:, kc, m * 128 : (m + 1) * 128],
                                     mov, start=(kc == 0), stop=(kc == 7))
                    ctx.inc(i, "tp")
                    ctx.wait(v, "tp")
                    xc = sb["xf"][:, m, 1 + ng * 512 : 1 + ng * 512 + 512]
                    i = v.scalar_tensor_tensor(xc, pb, sb["boutc"][:, m : m + 1],
                                               xc, op0=ALU.add, op1=ALU.add)
                    ctx.inc(i, "vp")
                    ctx.wait(t, "vp")




def build_nc():
    nc = bass.Bass("TRN2", target_bir_lowering=False, debug=False,
                  detect_race_conditions=bool(RACECHECK))
    ctx = Ctx(nc)

    din = {}
    spec = {
        "x_fm": ([8, 128, TOK], F32),
        "xh_fm": ([8, 128, NHT * 128 + 1], F32),
        "wqkv_a": ([2, 8, 128, 1536], F32),
        "wo_a": ([2, 4, 128, 1024], F32),
        "w1_a": ([2, 8, 128, 2 * HIDP], F32),
        "w2_a": ([2, KHID, 128, 1024], F32),
        "b1_a": ([128, 2, 2 * KHID], F32),
        "b2_a": ([128, 2, 8], F32),
        "lgb_a": ([128, 2, 2, 8], F32),
        "qs_bc": ([128, 2, 64], F32),
        "ks_bc": ([128, 2, 64], F32),
        "wqg_a": ([8, 128, KG * 64], F32),
        "wkvg_a": ([8, 128, 128], F32),
        "wog_a": ([8, 128, 1024], F32),
        "boutg_a": ([128, 8], F32),
        "corr_rel": ([128, KG, 2, 128], F32),
        "rel31_bc": ([128, KG], F32),
        "cmask": ([128, 2, 128], F32),
        "tri": ([128, 2, 128], BF16),
        "poison_row": ([1, ATT * 128], F32),
        "rot_t": ([4, ATT * 128, 64], BF16),
        "ident": ([128, 128], F32),
        "role_col": ([128, 1], F32),
        "nrole_col": ([128, 1], F32),
        "ones_col": ([128, 1], F32),
    }
    for name, (shape, dt) in spec.items():
        din[name] = nc.dram_tensor(name, shape, dt, kind="ExternalInput")
    dout = nc.dram_tensor("y_fm", [8, 128, TOK], F32, kind="ExternalOutput")

    es = ExitStack()
    sb = ctx.sb
    def alloc(name, shape, dt):
        sb[name] = es.enter_context(nc.sbuf_tensor(name, shape, dt))

    alloc("xf", [128, 8, TOK + 1], F32)
    alloc("lgb", [128, 2, 2, 8], F32)
    alloc("b1c", [128, 2, 2 * KHID], F32)
    alloc("b2c", [128, 2, 8], F32)
    alloc("boutc", [128, 8], F32)
    alloc("qs", [128, 2, 64], F32)
    alloc("ks", [128, 2, 64], F32)
    alloc("rel31", [128, KG], F32)
    alloc("role", [128, 1], F32)
    alloc("nrole", [128, 1], F32)
    alloc("onesb", [128, 1], BF16)
    alloc("cpool", [128, len(CONST_VALS)], F32)
    alloc("ones_row", [1, 128], F32)
    alloc("ones_row_bf", [1, 128], BF16)
    alloc("sb_tri", [128, 2, 128], BF16)
    alloc("sb_ident", [128, 128], BF16)
    sb["psA"] = es.enter_context(nc.psum_tensor("psA", [128, 2048], F32))
    sb["psB"] = es.enter_context(nc.psum_tensor("psB", [128, 2048], F32))

    for name in ["qd", "gd", "tp", "vp", "sp", "gp", "cc"]:
        ctx.sems[name] = Sem(es.enter_context(nc.semaphore(name)))

    ctx.dbg_out = dout
    for ci, cv in enumerate(CONST_VALS):
        i = ctx.v.memset(sb["cpool"][:, ci : ci + 1], cv)
        nc.const_aps.register(F32, cv, sb["cpool"][:, ci : ci + 1])
    ctx.inc(i, "vp")
    ph_load_const(ctx, din)
    i = ctx.v.memset(sb["ones_row"][:], 1.0)
    i = ctx.v.memset(sb["ones_row_bf"][:], 1.0)
    ctx.inc(i, "vp")
    dram = {
        "slab_snd": nc.dram_tensor("slab_snd", [8, 128, 641], BF16),
        "slab_rcv": nc.dram_tensor("slab_rcv", [2, 8, 128, 641], BF16),
        "x1_snd": nc.dram_tensor("x1_snd", [128, 8, TOK], BF16),
        "x1_rcv": nc.dram_tensor("x1_rcv", [2, 128, 8, TOK], BF16),
        "og_snd": nc.dram_tensor("og_snd", [128, 4, TOK], BF16),
        "og_rcv": nc.dram_tensor("og_rcv", [2, 128, 4, TOK], BF16),
        "x2_spill": nc.dram_tensor("x2_spill", [8, 128, TOK], F32),
    }
    wbf = {}
    for nm, shape in [("wqkv_a", [2, 8, 128, 1536]), ("wo_a", [2, 4, 128, 1024]),
                      ("w1_a", [2, 8, 128, 2 * HIDP]), ("w2_a", [2, KHID, 128, 1024]),
                      ("wqg_a", [8, 128, KG * 64]), ("wkvg_a", [8, 128, 128]),
                      ("wog_a", [8, 128, 1024])]:
        wbf[nm] = nc.dram_tensor(nm + "_bf", shape, BF16)
    ctx.wbf = wbf
    for nm in ["wqkv_a", "wo_a", "w1_a", "w2_a", "wqg_a", "wkvg_a", "wog_a"]:
        i = ctx.g.dma_start(out=wbf[nm].ap(), in_=din[nm].ap())
        ctx.inc(i, "gd", 16)
    ctx.wbf_done = ctx.sems["gd"].n
    done = False
    with nc.sbuf_tensor("c0_slab", [128, 8, NHT * 128 + 1], BF16) as slab0, \
         nc.sbuf_tensor("c0_rott", [128, 4, ATT, 64], BF16) as rott0:
        sb["slab"], sb["rott"] = slab0, rott0
        ph_outer_ln1(ctx, din)
        done = DEBUG_STOP == "ln1"
        if not done:
            for l in range(2):
                if l > 0:
                    ph_slab_ag(ctx, dram)
                ph_ffn_layer(ctx, din, dram, l, 0, dbg=DEBUG_STOP)
                if DEBUG_STOP in (f"c0l{l}mid", f"c0l{l}"):
                    done = True
                    break
        if not done:
            ph_call_end(ctx, din, "x_fm", dram, 0)
            done = DEBUG_STOP == "x1"
        del sb["slab"], sb["rott"]
    if not done:
        ph_global(ctx, din, dram)
        done = DEBUG_STOP == "x2"
    if not done:
        with nc.sbuf_tensor("c1_slab", [128, 8, NHT * 128 + 1], BF16) as slab1, \
             nc.sbuf_tensor("c1_rott", [128, 4, ATT, 64], BF16) as rott1:
            sb["slab"], sb["rott"] = slab1, rott1
            ph_outer_ln1(ctx, din, with_halo=False, spill=dram["x2_spill"], tag="ln3")
            if DEBUG_STOP == "y3":
                done = True
            if not done:
                for l in range(2):
                    ph_slab_ag(ctx, dram)
                    ph_ffn_layer(ctx, din, dram, l, 1, dbg=DEBUG_STOP)
                    if DEBUG_STOP in (f"c1l{l}mid", f"c1l{l}"):
                        done = True
                        break
            if not done:
                ph_call_end(ctx, din, "x2_spill", dram, 1)
            del sb["slab"], sb["rott"]
    ph_debug_dump(ctx, dout)

    es.close()

    with nc.Block() as block:

        @block.tensor
        def _(eng):
            ctx.t.replay(eng)

        @block.vector
        def _(eng):
            ctx.v.replay(eng)

        @block.scalar
        def _(eng):
            ctx.s.replay(eng)

        @block.gpsimd
        def _(eng):
            ctx.g.replay(eng)

        @block.sync
        def _(eng):
            ctx.q.replay(eng)

    return nc, din, dout




def kernel(**inputs):
    nc, _, _ = build_nc()
    in_maps = [_prep_core(c, inputs) for c in range(NC)]
    res = run_bass_kernel_spmd(nc, in_maps, core_ids=list(range(NC)))
    out = np.empty((B, N, D), np.float32)
    for c in range(NC):
        y = res.results[c]["y_fm"]
        b, r = c // 2, c % 2
        out[b, r * TOK : (r + 1) * TOK] = y.transpose(2, 0, 1).reshape(TOK, D)
    return out



TRI_PREV, TRI_DIAG = 0, 1


def ph_slab_ag(ctx, dram):
    sb, g, v = ctx.sb, ctx.g, ctx.v
    ctx.wait(g, "vp")
    ctx.wait(g, "tp")
    i = g.dma_start(out=dram["slab_snd"].ap(), in_=sb["xf"][:, :, TOK + 1 - 641 : TOK + 1])
    ctx.inc(i, "gd", 16)
    ctx.wait(g, "gd")
    i = g.collective_compute(
        "AllGather", ALU.bypass,
        replica_groups=[[0, 1], [2, 3], [4, 5], [6, 7]],
        ins=[dram["slab_snd"].ap().opt()],
        outs=[dram["slab_rcv"].ap().opt()],
    )
    ctx.inc(i, "cc")
    ctx.wait(g, "cc")
    i = g.dma_start(out=sb["slab"][:], in_=dram["slab_rcv"].ap()[0])
    ctx.inc(i, "gd", 16)
    ctx.wait(v, "gd")
    i = v.tensor_scalar_mul(sb["slab"][:], sb["slab"][:], sb["role"][:])
    ctx.inc(i, "vp")
    ctx.wait(v, "vp")
    i = v.tensor_copy(sb["xf"][:, :, 0:1], sb["slab"][:, :, 640:641])
    ctx.inc(i, "vp")


def _qk_vec(ctx, bufs, par, ti, is_q, slot, L, mm_done):
    t, v, s = ctx.t, ctx.v, ctx.s
    sb = ctx.sb
    ps = bufs["psq"] if is_q else bufs["psk"]
    tblc = sb["rott"][:, 0 if is_q else 2, ti, :]
    tbls = sb["rott"][:, 1 if is_q else 3, ti, :]
    bsq, rss, rn = bufs["bsq"], bufs["rss"], bufs["rn"]
    qn, tmp1, tmp2 = bufs["qn"], bufs["tmp1"], bufs["tmp2"]
    qrot = bufs["qrot"] if is_q else bufs["qrotk"]
    ps3 = ps.rearrange("p (h d) -> p h d", h=8)
    ctx.wait(s, "tp", mm_done)
    i = s.activation(bsq[:], ps, AF.Square)
    ctx.inc(i, "sp"); ctx.wait(v, "sp")
    i = v.reduce_sum(rss[:], bsq[:].rearrange("p (h d) -> p h d", h=8), axis=AX.X)
    ctx.inc(i, "vp")
    ctx.wait(s, "vp")
    i = s.activation(rn[:], rss[:], AF.Ln, bias=1e-24)
    ctx.inc(i, "sp")
    i = s.activation(rn[:], rn[:], AF.Exp, scale=-0.5)
    ctx.inc(i, "sp"); ctx.wait(v, "sp")
    rnb = rn[:].unsqueeze(2).broadcast_to([128, 8, 64])
    i = v.tensor_mul(qn[:].rearrange("p (h d) -> p h d", h=8), ps3, rnb)
    ctx.inc(i, "vp"); ctx.wait(v, "vp")
    cb = tblc.unsqueeze(1).broadcast_to([128, 8, 64])
    i = v.tensor_mul(tmp1[:].rearrange("p (h d) -> p h d", h=8),
                     qn[:].rearrange("p (h d) -> p h d", h=8), cb)
    ctx.inc(i, "vp")
    qn_swap = bass.AP(qn, 32, [qn.ap().ap[0], [64, 8], [-32, 2], [1, 32]])
    i = v.tensor_mul(tmp2[:].rearrange("p (h s d) -> p h s d", h=8, s=2),
                     qn_swap,
                     tbls.rearrange("p (s d) -> p s d", s=2)
                     .unsqueeze(1).broadcast_to([128, 8, 2, 32]))
    ctx.inc(i, "vp"); ctx.wait(v, "vp")
    i = v.tensor_add(qrot[:], tmp1[:], tmp2[:])
    ctx.inc(i, "vp"); ctx.wait(v, "vp")


def _qk_trans(ctx, bufs, ti, is_q, slot):
    t, v = ctx.t, ctx.v
    sb = ctx.sb
    qrot = bufs["qrot"] if is_q else bufs["qrotk"]
    pst = bufs["pst_q"] if is_q else bufs["pst_k"]
    ctx.wait(t, "vp")
    for p in range(4):
        i = t.matmul(pst[:, p * 128 : (p + 1) * 128],
                     qrot[:, p * 128 : (p + 1) * 128], sb["sb_ident"][:],
                     start=(p == 0), stop=(p == 3), is_transpose=True)
    ctx.inc(i, "tp")
    ctx.wait(v, "tp")
    pair = bufs["qpair"] if is_q else bufs["kpair"]
    idx = (slot + 1) if is_q else ti
    i = v.tensor_copy(pair[:, :, idx, :],
                      pst.rearrange("p (a b) -> p a b", a=4))
    ctx.inc(i, "vp")


def ph_ffn_layer(ctx, din, dram, l, call_idx, dbg=None):
    nc, sb = ctx.nc, ctx.sb
    t, v, s, g, q = ctx.t, ctx.v, ctx.s, ctx.g, ctx.q
    par = l % 2
    psA, psB = sb["psA"], sb["psB"]
    LTAG = f"c{call_idx}l{l}"

    with nc.sbuf_tensor(f"{LTAG}_qpair", [128, 4, 17, 128], BF16) as qpair, \
         nc.sbuf_tensor(f"{LTAG}_kpair", [128, 4, ATT, 128], BF16) as kpair, \
         nc.sbuf_tensor(f"{LTAG}_v66", [128, ATT, 8, 65], BF16) as v66, \
         nc.sbuf_tensor(f"{LTAG}_Kkb", [128, 4, ATT], F32) as Kkb, \
         nc.sbuf_tensor(f"{LTAG}_Kctx", [128, 4, 17], BF16) as Kctx, \
         nc.sbuf_tensor(f"{LTAG}_oall", [128, 4, 17, 128], BF16) as o_all:

        with nc.sbuf_tensor(f"{LTAG}_wqkv", [128, 8, 1536], BF16) as wqkv_b, \
             nc.sbuf_tensor(f"{LTAG}_xbf", [128, 2, 8, 129], BF16) as xbf, \
             nc.sbuf_tensor(f"{LTAG}_bsq", [128, 512], BF16) as bsq, \
             nc.sbuf_tensor(f"{LTAG}_rss", [128, 8], F32) as rss, \
             nc.sbuf_tensor(f"{LTAG}_rn", [128, 8], F32) as rn, \
             nc.sbuf_tensor(f"{LTAG}_qn", [128, 512], F32) as qn, \
             nc.sbuf_tensor(f"{LTAG}_t1", [128, 512], F32) as tmp1, \
             nc.sbuf_tensor(f"{LTAG}_t2", [128, 512], F32) as tmp2, \
             nc.sbuf_tensor(f"{LTAG}_qrot", [128, 512], BF16) as qrot, \
             nc.sbuf_tensor(f"{LTAG}_qrotk", [128, 512], BF16) as qrotk:

            ctx.wait(g, "tp")
            ctx.wait(g, "vp")
            ctx.wait(g, "sp")
            ctx.wait(q, "gd", ctx.wbf_done)
            ctx.wait(q, "tp")
            ctx.wait(q, "vp")
            ctx.wait(q, "sp")
            i = q.dma_start(out=sb["rott"][:],
                            in_=din["rot_t"].ap()
                            .rearrange("t (a p) d -> p t a d", p=128))
            ctx.inc(i, "qd", 16)
            ctx.wait(v, "qd")
            for tbl, sct in ((0, "qs"), (1, "qs"), (2, "ks"), (3, "ks")):
                st = sb[sct]
                if tbl % 2:
                    sap = bass.AP(st, par * 64 + 32,
                                  [st.ap().ap[0], [0, ATT], [-32, 2], [1, 32]])
                    i = v.tensor_mul(
                        sb["rott"][:, tbl, :, :]
                        .rearrange("p t (s d) -> p t s d", s=2),
                        sb["rott"][:, tbl, :, :]
                        .rearrange("p t (s d) -> p t s d", s=2), sap)
                else:
                    i = v.tensor_mul(sb["rott"][:, tbl, :, :],
                                     sb["rott"][:, tbl, :, :],
                                     st[:, par, :].unsqueeze(1)
                                     .broadcast_to([128, ATT, 64]))
                ctx.inc(i, "vp")
            i = q.dma_start(out=wqkv_b[:],
                            in_=ctx.wbf["wqkv_a"].ap()[par].rearrange("c p j -> p c j"))
            ctx.inc(i, "qd", 16)
            bufs = dict(
                psq=psA[:, 0:512], psk=psA[:, 512:1024], psv=psA[:, 1024:1536],
                pst_q=psB[:, 0:256].bitcast(BF16), pst_k=psB[:, 512:768].bitcast(BF16),
                bsq=bsq, rss=rss, rn=rn, qn=qn, tmp1=tmp1, tmp2=tmp2,
                qrot=qrot, qrotk=qrotk, qpair=qpair, kpair=kpair,
            )
            i = v.memset(v66[:, :, :, 64:65], 1.0)
            ctx.inc(i, "vp")
            ctx.wait(t, "qd")

            slots = list(range(NT)) + list(range(-NHT, 0))
            nslots = len(slots)
            vfree_v = [None, None]
            vfree_s = [None, None]
            cast_cnt = [0] * (nslots + 1)
            mm_done = [0] * (nslots + 1)

            def emit_cast(sj):
                if slots[sj] >= 0:
                    ctx.wait(v, "tp", mm_done[sj - 2] if sj >= 2 else 0)
                    i = v.tensor_copy(
                        xbf[:, sj % 2, :, :],
                        sb["xf"][:, :, slots[sj] * 128 : slots[sj] * 128 + 129])
                    ctx.inc(i, "vp")
                cast_cnt[sj] = ctx.sems["vp"].n

            def emit_mm(sj):
                slot_j = slots[sj]
                ps = psA if sj % 2 == 0 else psB
                ctx.wait(t, "vp", cast_cnt[sj])
                if vfree_v[sj % 2] is not None:
                    ctx.wait(t, "vp", vfree_v[sj % 2])
                    ctx.wait(t, "sp", vfree_s[sj % 2])
                for ns in range(3):
                    for c in range(8):
                        if slot_j >= 0:
                            stat = (xbf[:, sj % 2, c, 1:129] if c < 4
                                    else xbf[:, sj % 2, c, 0:128])
                        else:
                            off = 128 * slot_j + 640
                            stat = (sb["slab"][:, c, off + 1 : off + 129]
                                    if c < 4
                                    else sb["slab"][:, c, off : off + 128])
                        i = t.matmul(ps[:, ns * 512 : ns * 512 + 512], stat,
                                     wqkv_b[:, c, ns * 512 : (ns + 1) * 512],
                                     start=(c == 0), stop=(c == 7))
                    ctx.inc(i, "tp")
                mm_done[sj] = ctx.sems["tp"].n

            emit_cast(0)
            emit_mm(0)
            for si, slot in enumerate(slots):
                ti = slot + NHT
                sl = si % 2
                ps = psA if sl == 0 else psB
                bufs["psq"] = ps[:, 0:512]
                bufs["psk"] = ps[:, 512:1024]
                bufs["psv"] = ps[:, 1024:1536]
                bufs["pst_q"] = psA[:, 1536:1792].bitcast(BF16)
                bufs["pst_k"] = psB[:, 1536:1792].bitcast(BF16)
                if si + 1 < nslots:
                    emit_cast(si + 1)
                ctx.wait(v, "tp", mm_done[si])
                i = v.tensor_copy(v66[:, ti, :, 0:64],
                                  bufs["psv"].rearrange("p (h d) -> p h d", h=8))
                ctx.inc(i, "vp")
                if slot >= -1:
                    _qk_vec(ctx, bufs, par, ti, True, slot, l, mm_done[si])
                _qk_vec(ctx, bufs, par, ti, False, slot, l, mm_done[si])
                vfree_v[sl] = ctx.sems["vp"].n
                vfree_s[sl] = ctx.sems["sp"].n
                if si + 1 < nslots:
                    emit_mm(si + 1)
                if slot >= -1:
                    _qk_trans(ctx, bufs, ti, True, slot)
                _qk_trans(ctx, bufs, ti, False, slot)

            ctx.wait(v, "tp")
            ctx.wait(v, "vp")
            for p in range(4):
                i = v.tensor_reduce(Kkb[:, p, :], kpair[:, p, :, :], axis=AX.X,
                                    op=ALU.max, apply_absolute_value=True)
                ctx.inc(i, "vp")
            ctx.wait(v, "vp")
            for t17 in range(17):
                i = v.tensor_reduce(Kctx[:, :, t17 : t17 + 1], Kkb[:, :, t17 : t17 + 5],
                                    axis=AX.X, op=ALU.max)
                ctx.inc(i, "vp")

        with nc.sbuf_tensor(f"{LTAG}_q65", [128, 2, 17 * 128], BF16) as q65w, \
             nc.sbuf_tensor(f"{LTAG}_qabs", [128, 17 * 128], BF16) as qabs, \
             nc.sbuf_tensor(f"{LTAG}_k65", [128, 2, ATT * 128], BF16) as k65w, \
             nc.sbuf_tensor(f"{LTAG}_pt", [128, 2048], BF16) as PT, \
             nc.sbuf_tensor(f"{LTAG}_rec", [1, 512], F32) as rec, \
             nc.sbuf_tensor(f"{LTAG}_recb", [128, 512], F32) as recb, \
             nc.sbuf_tensor(f"{LTAG}_wo", [128, 4, 1024], BF16) as wo_b:

            ctx.wait(g, "vp")
            ctx.wait(g, "tp")
            ctx.wait(g, "sp")
            ctx.wait(q, "tp")
            ctx.wait(q, "vp")
            ctx.wait(q, "sp")
            i = q.dma_start(out=wo_b[:],
                            in_=ctx.wbf["wo_a"].ap()[par].rearrange("k p j -> p k j"))
            ctx.inc(i, "qd", 16)
            i = g.dma_start(out=k65w[64:65, 0, :], in_=din["poison_row"].ap())
            ctx.inc(i, "gd", 16)
            i = g.dma_start(out=k65w[64:65, 1, :], in_=din["poison_row"].ap())
            ctx.inc(i, "gd", 16)

            for p in range(4):
                ctx.wait(v, "tp")
                ctx.wait(v, "sp")
                ctx.wait(s, "tp")
                ctx.wait(s, "vp")
                i = s.activation(qabs[:], qpair[:, p, :, :], AF.Abs)
                ctx.inc(i, "sp")
                ctx.wait(t, "sp")
                ctx.wait(t, "vp")
                for hh in range(2):
                    for t17 in range(17):
                        pbb = (psA[0:1, t17 * 128 : (t17 + 1) * 128] if t17 < 16
                               else psB[0:1, 0:128])
                        i = t.matmul(pbb,
                                     Kctx[64 * hh : 64 * hh + 64, p, t17 : t17 + 1],
                                     qabs[64 * hh : 64 * hh + 64,
                                          t17 * 128 : (t17 + 1) * 128],
                                     start=True, stop=True)
                        ctx.inc(i, "tp")
                    ctx.wait(v, "tp")
                    i = v.tensor_scalar_mul(q65w[64:65, hh, 0:2048],
                                            psA[0:1, 0:2048], -1.0)
                    ctx.inc(i, "vp")
                    i = v.tensor_scalar_mul(q65w[64:65, hh, 2048:2176],
                                            psB[0:1, 0:128], -1.0)
                    ctx.inc(i, "vp")
                    ctx.wait(t, "vp")
                ctx.wait(v, "gd")
                ctx.wait(v, "tp")
                i = v.tensor_copy(q65w[0:64, 0, :], qpair[0:64, p, :, :])
                ctx.inc(i, "vp")
                i = v.tensor_copy(q65w[0:64, 1, :], qpair[64:128, p, :, :])
                ctx.inc(i, "vp")
                i = v.tensor_copy(k65w[0:64, 0, :], kpair[0:64, p, :, :])
                ctx.inc(i, "vp")
                i = v.tensor_copy(k65w[0:64, 1, :], kpair[64:128, p, :, :])
                ctx.inc(i, "vp")
                ctx.wait(t, "vp")

                for w in range(-1, NW):
                    if PARANOID:
                        _pbar(ctx)
                    if w < 0:
                        kbs = ([("prev", 3, -5)]
                               + [("own", c, c - 4) for c in range(4)])
                        qcol0, qw = 0, 128
                        batches = [kbs]
                    else:
                        kbs = ([("prev", c, 4 * w - 4 + c) for c in range(3, -1, -1)]
                               + [("own", c, 4 * w + c) for c in range(4)])
                        qcol0, qw = (4 * w + 1) * 128, 512
                        batches = [kbs[0:4], kbs[4:8]]
                    for hh in range(2):
                        hgl = 2 * p + hh
                        av_first = True
                        bdone = 0
                        for bz, batch in enumerate(batches):
                            ctx.wait(t, "sp")
                            for bi, (kind, c, kb) in enumerate(batch):
                                sl = psA[:, bi * qw : (bi + 1) * qw]
                                if qw == 512:
                                    st, sp_ = True, True
                                else:
                                    st = bi % 4 == 0
                                    sp_ = (bi % 4 == 3) or (bi == len(batch) - 1)
                                i = t.matmul(
                                    sl,
                                    k65w[0:65, hh, (kb + 5) * 128 : (kb + 6) * 128],
                                    q65w[0:65, hh, qcol0 : qcol0 + qw],
                                    start=st, stop=sp_)
                            ctx.inc(i, "tp")
                            ctx.wait(v, "tp")
                            nmask = 0
                            for bi, (kind, c, kb) in enumerate(batch):
                                if w < 0:
                                    if kind == "prev":
                                        tri, qoff = TRI_PREV, 0
                                    elif c == 3:
                                        tri, qoff = TRI_DIAG, 0
                                    else:
                                        continue
                                else:
                                    tri = TRI_PREV if kind == "prev" else TRI_DIAG
                                    qoff = c * 128
                                sl = psA[:, bi * qw + qoff : bi * qw + qoff + 128]
                                i = v.tensor_add(sl, sl, sb["sb_tri"][:, tri, :])
                                ctx.inc(i, "vp")
                                nmask += 1
                            ctx.wait(s, "vp")
                            ctx.wait(s, "tp")
                            bw = len(batch) * qw
                            i = s.activation(PT[:, 0:bw], psA[:, 0:bw], AF.Exp,
                                             scale=8.0)
                            ctx.inc(i, "sp")
                            ctx.wait(t, "sp")
                            for bi, (kind, c, kb) in enumerate(batch):
                                if w < 0:
                                    a0_, a1_ = 0, 128
                                elif kind == "prev":
                                    a0_, a1_ = 0, (c + 1) * 128
                                else:
                                    a0_, a1_ = c * 128, 512
                                last = (bz == len(batches) - 1
                                        and bi == len(batch) - 1)
                                i = t.matmul(psB[0:65, a0_:a1_],
                                             v66[:, kb + 5, hgl, :],
                                             PT[:, bi * qw + a0_ : bi * qw + a1_],
                                             start=av_first, stop=last)
                                av_first = False
                            ctx.inc(i, "tp")
                        ctx.wait(s, "tp")
                        i = s.activation(rec[:, 0:qw], psB[64:65, 0:qw], AF.Ln)
                        ctx.inc(i, "sp")
                        i = s.activation(rec[:, 0:qw], rec[:, 0:qw], AF.Exp,
                                         scale=-1.0)
                        ctx.inc(i, "sp")
                        ctx.wait(t, "sp")
                        i = t.matmul(psB[0:64, 512 : 512 + qw],
                                     sb["ones_row"][0:1, 0:64], rec[0:1, 0:qw],
                                     start=True, stop=True)
                        ctx.inc(i, "tp")
                        ctx.wait(v, "tp")
                        i = v.tensor_copy(recb[0:64, 0:qw], psB[0:64, 512 : 512 + qw])
                        ctx.inc(i, "vp")
                        ctx.wait(v, "vp")
                        i = v.tensor_mul(
                            o_all[64 * hh : 64 * hh + 64, p,
                                  (qcol0 // 128) : (qcol0 // 128) + qw // 128, :]
                            .rearrange("p a b -> p (a b)"),
                            psB[0:64, 0:qw], recb[0:64, 0:qw])
                        ctx.inc(i, "vp")
                        ctx.wait(t, "vp")

            if dbg == f"{LTAG}o":
                ctx.wait(ctx.g, "vp")
                ctx.wait(ctx.g, "tp")
                i = ctx.g.dma_start(
                    out=ctx.dbg_out.ap().rearrange("c p t -> p c t")[:, 0:4, :],
                    in_=o_all[:, :, 0:16, :])
                ctx.inc(i, "gd", 16)
                i = ctx.g.dma_start(
                    out=ctx.dbg_out.ap().rearrange("c p t -> p c t")[:, 4:8, 0:128],
                    in_=o_all[:, :, 16:17, :])
                ctx.inc(i, "gd", 16)
                ctx.wait(ctx.g, "gd")
                return
            ctx.wait(t, "vp")
            ctx.wait(t, "qd")
            for cg in range(5):
                wdt = 128 if cg == 0 else 512
                scol = 0 if cg == 0 else (4 * (cg - 1) + 1) * 128
                for m in range(8):
                    pb = psA[:, (m % 2) * 512 : (m % 2) * 512 + wdt]
                    for kc in range(4):
                        i = t.matmul(pb, wo_b[:, kc, m * 128 : (m + 1) * 128],
                                     o_all[:, kc, scol // 128 : (scol + wdt) // 128, :]
                                     .rearrange("p a b -> p (a b)"),
                                     start=(kc == 0), stop=(kc == 3))
                    ctx.inc(i, "tp")
                    ctx.wait(v, "tp")
                    if cg == 0:
                        dst = sb["slab"][:, m, 513:641]
                        i = v.tensor_add(dst, pb, dst)
                    else:
                        dst = sb["xf"][:, m, scol - 127 : scol - 127 + wdt]
                        i = v.tensor_add(dst, pb, dst)
                    ctx.inc(i, "vp")
                    ctx.wait(t, "vp")
            ctx.wait(v, "vp")
            i = v.tensor_copy(sb["xf"][:, :, 0:1], sb["slab"][:, :, 640:641])
            ctx.inc(i, "vp")

    if dbg == f"{LTAG}mid":
        return

    _ffn_part(ctx, din, l, call_idx)


def _ffn_part(ctx, din, l, call_idx):
    nc, sb = ctx.nc, ctx.sb
    t, v, s, g, q = ctx.t, ctx.v, ctx.s, ctx.g, ctx.q
    par = l % 2
    psA, psB = sb["psA"], sb["psB"]
    LTAG = f"c{call_idx}l{l}"
    xf = sb["xf"]

    def shifted(c, c0, w):
        if c < 4:
            return xf[:, c, c0 + 1 : c0 + 1 + w]
        return xf[:, c, c0 : c0 + w]

    with nc.sbuf_tensor(f"{LTAG}_msb", [128, TOK], BF16) as msb, \
         nc.sbuf_tensor(f"{LTAG}_sbb", [128, TOK], F32) as sbb, \
         nc.sbuf_tensor(f"{LTAG}_cast", [128, TOK], BF16) as castb, \
         nc.sbuf_tensor(f"{LTAG}_sq2", [128, TOK], BF16) as sq2, \
         nc.sbuf_tensor(f"{LTAG}_hn", [128, 8, QT], BF16) as hnorm, \
         nc.sbuf_tensor(f"{LTAG}_hg", [128, KHID, QT], BF16) as hgate, \
         nc.sbuf_tensor(f"{LTAG}_w1r", [128, 2, 2, 8, 256], BF16) as w1r, \
         nc.sbuf_tensor(f"{LTAG}_w2r", [128, 1, KHID, 128], BF16) as w2r, \
         nc.sbuf_tensor(f"{LTAG}_ab", [128, 2, 512], BF16) as aabf, \
         nc.sbuf_tensor(f"{LTAG}_gb", [128, 2, 512], BF16) as ggbf, \
         nc.sbuf_tensor(f"{LTAG}_xms", [128, 8, 1], F32) as xmstash:

        with nc.sbuf_tensor(f"{LTAG}_rows", [1, 1, TOK], F32) as rows, \
             nc.sbuf_tensor(f"{LTAG}_msr", [1, TOK], BF16) as msr:
            ctx.wait(v, "vp")
            ctx.wait(v, "tp")
            ctx.wait(v, "sp")
            i = v.tensor_copy(xmstash[:], sb["xf"][:, :, QT : QT + 1])
            ctx.inc(i, "vp")
            ctx.wait(v, "vp")
            ctx.wait(g, "vp")
            ctx.wait(g, "tp")
            ctx.wait(g, "sp")
            for c in range(8):
                i = v.tensor_copy(castb[:], shifted(c, 0, TOK))
                ctx.inc(i, "vp"); ctx.wait(v, "vp")
                i = v.tensor_mul(sq2[:], castb[:], castb[:])
                ctx.inc(i, "vp")
                ctx.wait(t, "vp")
                for ns in range(4):
                    sl = slice(ns * 512, ns * 512 + 512)
                    i = t.matmul(psA[0:1, sl], sb["onesb"][:], castb[:, sl],
                                 start=(c == 0), stop=(c == 7))
                    i = t.matmul(psB[0:1, sl], sb["onesb"][:], sq2[:, sl],
                                 start=(c == 0), stop=(c == 7))
                    ctx.inc(i, "tp")
                ctx.wait(v, "tp")
            inv = 1.0 / 1024.0
            mean = psA[32:33, 0:TOK]
            var = psB[32:33, 0:TOK]
            m2 = msr[0:1, :]
            i = v.tensor_scalar(mean, psA[0:1, 0:TOK], inv, None, op0=ALU.mult)
            ctx.inc(i, "vp")
            i = v.tensor_scalar(var, psB[0:1, 0:TOK], inv, None, op0=ALU.mult)
            ctx.inc(i, "vp")
            ctx.wait(s, "vp")
            i = s.activation(m2, mean, AF.Square)
            ctx.inc(i, "sp"); ctx.wait(v, "sp")
            i = v.tensor_sub(var, var, m2)
            ctx.inc(i, "vp")
            ctx.wait(s, "vp")
            i = s.activation(var, var, AF.Sqrt, bias=1e-5)
            ctx.inc(i, "sp"); ctx.wait(v, "sp")
            i = v.reciprocal(rows[0:1, 0, :], var)
            ctx.inc(i, "vp"); ctx.wait(v, "vp")
            i = v.tensor_mul(msr[0:1, :], mean, rows[0:1, 0, :])
            ctx.inc(i, "vp"); ctx.wait(v, "vp")
            i = v.tensor_scalar_mul(msr[0:1, :], msr[0:1, :], -1.0)
            ctx.inc(i, "vp")
            ctx.wait(t, "vp")
            for ns in range(4):
                sl = slice(ns * 512, ns * 512 + 512)
                i = t.matmul(psB[:, sl], sb["ones_row_bf"][:], msr[0:1, sl],
                             start=True, stop=True)
            ctx.inc(i, "tp")
            ctx.wait(v, "tp")
            i = v.tensor_copy(msb[:], psB[:, 0:TOK])
            ctx.inc(i, "vp")
            ctx.wait(t, "vp")
            for ns in range(4):
                sl = slice(ns * 512, ns * 512 + 512)
                i = t.matmul(psA[:, sl], sb["ones_row"][:], rows[0:1, 0, sl],
                             start=True, stop=True)
            ctx.inc(i, "tp")
            ctx.wait(v, "tp")
            i = v.tensor_copy(sbb[:], psA[:, 0:TOK])
            ctx.inc(i, "vp")

        for qt in range(TOK // QT):
            q0 = qt * QT
            with nc.sbuf_tensor(f"{LTAG}_tmpf{qt}", [128, QT], F32) as tmpf:
                ctx.wait(v, "tp")
                for c in range(8):
                    if c >= 4 and qt > 0:
                        i = v.tensor_mul(tmpf[:, 0:1], xmstash[:, c, :],
                                         sbb[:, q0 : q0 + 1])
                        ctx.inc(i, "vp")
                        i = v.tensor_mul(tmpf[:, 1:QT],
                                         sb["xf"][:, c, q0 + 1 : q0 + QT],
                                         sbb[:, q0 + 1 : q0 + QT])
                        ctx.inc(i, "vp")
                    else:
                        i = v.tensor_mul(tmpf[:], shifted(c, q0, QT),
                                         sbb[:, q0 : q0 + QT])
                        ctx.inc(i, "vp")
                    ctx.wait(v, "vp")
                    i = v.tensor_add(tmpf[:], tmpf[:], msb[:, q0 : q0 + QT])
                    ctx.inc(i, "vp")
                    ctx.wait(s, "vp")
                    i = s.activation(hnorm[:, c, :], tmpf[:], AF.Identity,
                                     scale=sb["lgb"][:, par, 0, c : c + 1],
                                     bias=sb["lgb"][:, par, 1, c : c + 1])
                    ctx.inc(i, "sp")
                    ctx.wait(v, "sp")

            ctx.wait(t, "sp")
            nmt = KHID
            w1_vfree = [ctx.sems["vp"].n, ctx.sems["vp"].n]
            w1_sfree = [ctx.sems["sp"].n, ctx.sems["sp"].n]
            for m2 in range((nmt + 1) // 2):
                buf = m2 % 2
                mts = [2 * m2 + j for j in range(2) if 2 * m2 + j < nmt]
                ctx.wait(g, "tp", ctx.sems["tp"].n)
                ctx.wait(q, "tp")
                ctx.wait(q, "sp")
                i = q.dma_start(
                    out=w1r[:, buf, 0, :, 0 : 128 * len(mts)],
                    in_=ctx.wbf["w1_a"].ap()[par].rearrange("c p j -> p c j")
                    [:, :, 128 * mts[0] : 128 * (mts[-1] + 1)])
                ctx.inc(i, "qd", 16)
                i = q.dma_start(
                    out=w1r[:, buf, 1, :, 0 : 128 * len(mts)],
                    in_=ctx.wbf["w1_a"].ap()[par].rearrange("c p j -> p c j")
                    [:, :, 2816 + 128 * mts[0] : 2816 + 128 * (mts[-1] + 1)])
                ctx.inc(i, "qd", 16)
                ctx.wait(t, "qd")
                for mi, mt in enumerate(mts):
                    for n2 in range(QT // 512):
                        u = (mt * (QT // 512) + n2) % 2
                        nsl = slice(n2 * 512, n2 * 512 + 512)
                        ctx.wait(t, "vp", w1_vfree[u])
                        ctx.wait(t, "sp", w1_sfree[u])
                        for ag in range(2):
                            pb = psB[:, (2 * u + ag) * 512 : (2 * u + ag) * 512 + 512]
                            for kc in range(8):
                                i = t.matmul(
                                    pb, w1r[:, buf, ag, kc,
                                            mi * 128 : (mi + 1) * 128],
                                    hnorm[:, kc, nsl],
                                    start=(kc == 0), stop=(kc == 7))
                            ctx.inc(i, "tp")
                        ctx.wait(s, "tp")
                        ctx.wait(s, "vp", w1_vfree[u])
                        i = s.activation(ggbf[:, u, :], psB[:, (2 * u + 1) * 512 :
                                                            (2 * u + 1) * 512 + 512],
                                         AF.Gelu,
                                         bias=sb["b1c"][:, par, KHID + mt : KHID + mt + 1])
                        ctx.inc(i, "sp")
                        w1_sfree[u] = ctx.sems["sp"].n
                        ctx.wait(v, "tp")
                        i = v.tensor_scalar_add(aabf[:, u, :],
                                                psB[:, 2 * u * 512 : 2 * u * 512 + 512],
                                                sb["b1c"][:, par, mt : mt + 1])
                        ctx.inc(i, "vp")
                        ctx.wait(v, "sp")
                        ctx.wait(v, "vp")
                        i = v.tensor_mul(hgate[:, mt, nsl], aabf[:, u, :], ggbf[:, u, :])
                        ctx.inc(i, "vp")
                        w1_vfree[u] = ctx.sems["vp"].n

            for m in range(8):
                buf = 0
                ctx.wait(g, "tp", ctx.sems["tp"].n)
                ctx.wait(q, "tp")
                i = q.dma_start(
                    out=w2r[:, buf, :, :],
                    in_=ctx.wbf["w2_a"].ap()[par].rearrange("k p j -> p k j")
                    [:, :, m * 128 : (m + 1) * 128])
                ctx.inc(i, "qd", 16)
                ctx.wait(t, "qd")
                ctx.wait(t, "vp")
                for n2 in range(QT // 512):
                    nsl = slice(n2 * 512, n2 * 512 + 512)
                    pb = psB[:, 1024 + (n2 % 2) * 512 : 1024 + (n2 % 2) * 512 + 512]
                    for kc in range(KHID):
                        i = t.matmul(pb, w2r[:, buf, kc, :], hgate[:, kc, nsl],
                                     start=(kc == 0), stop=(kc == KHID - 1))
                    ctx.inc(i, "tp")
                    ctx.wait(v, "tp")
                    xc = sb["xf"][:, m, q0 + 1 + n2 * 512 : q0 + 1 + n2 * 512 + 512]
                    i = v.scalar_tensor_tensor(xc, pb, sb["b2c"][:, par, m : m + 1],
                                               xc, op0=ALU.add, op1=ALU.add)
                    ctx.inc(i, "vp")
                    ctx.wait(t, "vp")
